# version 33
# baseline (speedup 1.0000x reference)
import numpy as np
import ml_dtypes
from contextlib import ExitStack

import concourse.bass as bass
import concourse.mybir as mybir
from concourse.bass_utils import run_bass_kernel_spmd

F32 = mybir.dt.float32
BF16 = mybir.dt.bfloat16
AF = mybir.ActivationFunctionType
ALU = mybir.AluOpType
AX = mybir.AxisListType

T, D, NH, DK, E, HF = 4096, 1024, 16, 64, 8, 512
NT = T // 128
ND = D // 128
NB = T // 512
EPS = 1e-5
N_CORES = 8


class Sem:
    __slots__ = ("h", "n")

    def __init__(self, h):
        self.h = h
        self.n = 0

    def inc(self, k=1):
        self.n += k
        return self.n


def build(dbg=()):
    nc = bass.Bass("TRN2", target_bir_lowering=False)

    x_h = nc.dram_tensor("x", [T, D], F32, kind="ExternalInput")
    wqT_h = nc.dram_tensor("wqT", [D, D], F32, kind="ExternalInput")
    wkT_h = nc.dram_tensor("wkT", [D, D], F32, kind="ExternalInput")
    wvT_h = nc.dram_tensor("wvT", [D, D], F32, kind="ExternalInput")
    woT_h = nc.dram_tensor("woT", [D, D], F32, kind="ExternalInput")
    gw_h = nc.dram_tensor("gw", [D, E], F32, kind="ExternalInput")
    w13_h = nc.dram_tensor("w13", [E, D, 2 * HF], F32, kind="ExternalInput")
    w2_h = nc.dram_tensor("w2", [E, HF, D], F32, kind="ExternalInput")
    idf_h = nc.dram_tensor("idf", [128, 128], F32, kind="ExternalInput")
    idb_h = nc.dram_tensor("idb", [128, 128], BF16, kind="ExternalInput")
    cm_h = nc.dram_tensor("cmask", [128, 4 * 512], BF16, kind="ExternalInput")
    ones_h = nc.dram_tensor("ones", [1, 128], F32, kind="ExternalInput")
    out_h = nc.dram_tensor("out", [T, D], F32, kind="ExternalOutput")

    sfm_h = nc.dram_tensor("sfm", [E, T], F32)
    xnb_h = nc.dram_tensor("xnb", [T, D], BF16)
    rl_h = nc.dram_tensor("rl", [2, 512], F32)
    atT_h = nc.dram_tensor("atT", [D, T], BF16)
    w13b_h = nc.dram_tensor("w13b", [E, D, 2 * HF], BF16)
    w2b_h = nc.dram_tensor("w2b", [E, HF, D], BF16)

    dbg_h = {}
    for name, shape, dt in [
        ("xnT", [D, T], BF16),
        ("scale", [E, T], F32),
        ("attnT", [D, T], BF16),
        ("g0", [128, E * 4 * 512], BF16),
        ("y0", [128, 4 * 512], F32),
    ]:
        if name in dbg:
            dbg_h[name] = nc.dram_tensor("dbg_" + name, shape, dt, kind="ExternalOutput")

    P = dict(sync=[], gpsimd=[], scalar=[], vector=[], tensor=[])

    def add(en, fn, waits=(), inc=None, k=1):
        ws = tuple((s.h, int(v)) for (s, v) in waits if v > 0)
        if inc is not None:
            val = inc.inc(k)
        else:
            val = None

        def g(eng, ws=ws, fn=fn, inc=inc, k=k):
            for h, v in ws:
                eng.wait_ge(h, v)
            r = fn(eng)
            if inc is not None:
                r.then_inc(inc.h, k)

        P[en].append(g)
        return val

    with ExitStack() as ctx:
        def sbuf(name, shape, dt):
            return ctx.enter_context(nc.sbuf_tensor(name, shape, dt))

        def mksem(name):
            return Sem(ctx.enter_context(nc.semaphore(name)))

        s_pe, s_act, s_dve = mksem("s_pe"), mksem("s_act"), mksem("s_dve")
        s_cst = mksem("s_cst")
        s_x = [mksem("s_x0"), mksem("s_x1")]
        s_xnw = mksem("s_xnw")
        s_xt = mksem("s_xt")
        s_sf = [mksem("s_sf0"), mksem("s_sf1")]
        s_gt, s_wv, s_wo = mksem("s_gt"), mksem("s_wv"), mksem("s_wo")
        s_qk = mksem("s_qk")
        s_ao = [mksem("s_ao0"), mksem("s_ao1")]
        s_rb = [mksem("s_rb0"), mksem("s_rb1")]
        s_sbc = [mksem("s_sbc%d" % i) for i in range(3)]
        s_w13 = [mksem("s_w13a"), mksem("s_w13b")]
        s_w2 = [mksem("s_w2a"), mksem("s_w2b")]
        s_at = [mksem("s_at0"), mksem("s_at1")]
        s_x5 = [mksem("s_x5%d" % i) for i in range(4)]
        s_o = [mksem("s_o%d" % i) for i in range(4)]
        s_bnc = mksem("s_bnc")
        s_out = mksem("s_out")
        all_sems = [s_pe, s_act, s_dve, s_cst, s_gt, s_wv, s_wo, s_qk, s_xnw, s_xt,
                    s_bnc, s_out] + s_x + s_sf + s_ao + s_w13 + s_w2 + s_at + s_x5 + s_o + s_rb + s_sbc

        def barrier():
            snap = [(s, s.n) for s in all_sems]
            for en in ("sync", "gpsimd", "scalar", "vector", "tensor"):
                add(en, lambda e: e.nop(), waits=snap)

        xnT = sbuf("xnT", [128, ND * T], BF16)
        idf_sb = sbuf("idf_sb", [128, 128], F32)
        idb_sb = sbuf("idb_sb", [128, 128], BF16)
        ones_sb = sbuf("ones_sb", [1, 128], F32)
        cm_sb = sbuf("cm_sb", [128, 2048], BF16)

        wqp_cm = nc.sbuf_tensor("wqp", [128, 2048], BF16)
        wqp = wqp_cm.__enter__()
        wkp_cm = nc.sbuf_tensor("wkp", [128, 2048], BF16)
        wkp = wkp_cm.__enter__()
        v_aug_cm = nc.sbuf_tensor("v_aug", [128, NT * 1040], BF16)
        v_aug = v_aug_cm.__enter__()
        add("sync", lambda e: e.dma_start(out=idf_sb[:, :], in_=idf_h[:, :]), inc=s_cst, k=16)
        add("sync", lambda e: e.dma_start(out=idb_sb[:, :], in_=idb_h[:, :]), inc=s_cst, k=16)
        add("sync", lambda e: e.dma_start(out=ones_sb[:, :], in_=ones_h[:, :]), inc=s_cst, k=16)
        add("sync", lambda e: e.dma_start(out=cm_sb[:, :], in_=cm_h[:, :]), inc=s_cst, k=16)
        CST = (s_cst, 64)

        p012 = ExitStack()
        xbuf = p012.enter_context(nc.sbuf_tensor("xbuf", [128, 2 * D], F32))
        xn2 = p012.enter_context(nc.sbuf_tensor("xn2", [128, 2 * D], BF16))
        sq2 = p012.enter_context(nc.sbuf_tensor("sq2", [128, 2 * D], F32))
        stats = p012.enter_context(nc.sbuf_tensor("stats", [128, 18 * NT], F32))
        gp2 = p012.enter_context(nc.sbuf_tensor("gp2", [128, 2 * 8], F32))
        t82 = p012.enter_context(nc.sbuf_tensor("t82", [128, 2 * 8], F32))
        sc2 = p012.enter_context(nc.sbuf_tensor("sc2", [128, 2 * 8], F32))
        sst2 = p012.enter_context(nc.sbuf_tensor("sst2", [8, 2 * 128], F32))
        gsc = p012.enter_context(nc.sbuf_tensor("gsc", [128, 4 * NT], F32))
        gate_sb = p012.enter_context(nc.sbuf_tensor("gate_sb", [128, ND * E], BF16))
        wvb = p012.enter_context(nc.sbuf_tensor("wvb", [128, ND * 1024], BF16))

        psG = p012.enter_context(nc.psum_tensor("psG", [128, 512], F32))
        psS = p012.enter_context(nc.psum_tensor("psS", [8, 512], F32))
        psV0 = p012.enter_context(nc.psum_tensor("psV0", [128, 1024], F32))
        psV1 = p012.enter_context(nc.psum_tensor("psV1", [128, 1024], F32))
        psV = [psV0, psV1]

        add("scalar", lambda e: e.preload_act_table(AF.Sqrt))

        v_x = [0] * NT
        v_xn = [0] * NT
        v_xt = [0] * NT

        add("gpsimd", lambda e: e.dma_start(out=gate_sb[:, :].rearrange("p (dc ee) -> p dc ee", ee=E), in_=gw_h[:, :].rearrange("(dc p) ee -> p dc ee", p=128)),
            inc=s_gt, k=16)
        add("gpsimd", lambda e: e.dma_start(out=wvb[:, :].rearrange("p (dc o) -> p dc o", o=1024), in_=wvT_h[:, :].rearrange("(dc p) o -> p dc o", p=128)),
            inc=s_wv, k=16)
        v_wk_pair = [0] * (NH // 2)

        def emit_qk_cast(cc, waits=()):
            bqk = (cc % 2) * 1024
            add("gpsimd", lambda e, cc=cc: e.dma_start(
                out=wqp[:, bqk:bqk + 1024].rearrange("p (dc o) -> p dc o", o=128),
                in_=wqT_h[:, cc * 128:(cc + 1) * 128].rearrange("(dc p) o -> p dc o", p=128)),
                waits=waits, inc=s_qk, k=16)
            v_wk_pair[cc] = add("gpsimd", lambda e, cc=cc: e.dma_start(
                out=wkp[:, bqk:bqk + 1024].rearrange("p (dc o) -> p dc o", o=128),
                in_=wkT_h[:, cc * 128:(cc + 1) * 128].rearrange("(dc p) o -> p dc o", p=128)),
                inc=s_qk, k=16)

        emit_qk_cast(0)

        def emit_tblock(t_):
            for d in range(ND):
                add("sync", lambda e, d_=d, t__=t_:
                    e.dma_start(out=xnT[:, d_ * T + t__ * 512: d_ * T + t__ * 512 + 512],
                                in_=xnb_h[t__ * 512:(t__ + 1) * 512, d_ * 128:(d_ + 1) * 128],
                                transpose=True),
                    waits=([(s_xnw, 16 * 4 * (t_ + 1))] if d == 0 else ()), inc=s_xt, k=16)
            for ii in range(4 * t_, 4 * t_ + 4):
                v_xt[ii] = s_xt.n

        for i in range(NT):
            b = i % 2
            xs = xbuf[:, b * D:(b + 1) * D]
            w = []
            if i >= 2:
                w = [(s_dve, v_xn[i - 2])]
            v_x[i] = add("sync", lambda e, o=xs, i_=i: e.dma_start(out=o, in_=x_h[i_ * 128:(i_ + 1) * 128, :]),
                         waits=w, inc=s_x[b], k=16)
            c0 = 18 * i
            add("vector", lambda e, o=stats[:, c0:c0 + 6], s=xs: e.bn_stats(o, s[:, 0:512]),
                waits=[(s_x[b], v_x[i])], inc=s_dve)
            v_bn = add("vector", lambda e, o=stats[:, c0 + 6:c0 + 12], s=xs: e.bn_stats(o, s[:, 512:1024]),
                       inc=s_dve)
            v_ag = add("vector", lambda e, o=stats[:, c0 + 12:c0 + 14], s=stats[:, c0:c0 + 12]: e.bn_aggr(o, s),
                       waits=[(s_dve, v_bn)], inc=s_dve)
            v_var = add("vector", lambda e, o=stats[:, c0 + 14:c0 + 15], s=stats[:, c0 + 13:c0 + 14]:
                        e.tensor_scalar(out=o, in0=s, scalar1=EPS, scalar2=None, op0=ALU.add),
                        waits=[(s_dve, v_ag)], inc=s_dve)
            v_std = add("scalar", lambda e, o=stats[:, c0 + 15:c0 + 16], s=stats[:, c0 + 14:c0 + 15]:
                        e.activation(out=o, in_=s, func=AF.Sqrt),
                        waits=[(s_dve, v_var)], inc=s_act)
            v_rstd = add("vector", lambda e, o=stats[:, c0 + 16:c0 + 17], s=stats[:, c0 + 15:c0 + 16]: e.reciprocal(o, s),
                         waits=[(s_act, v_std)], inc=s_dve)
            w = [(s_dve, v_rstd)]
            if i >= 2:
                w.append((s_xnw, 16 * (i - 1)))
            v_xn[i] = add("vector", lambda e, o=xn2[:, b * D:(b + 1) * D], s=xs,
                          m_=stats[:, c0 + 12:c0 + 13], r=stats[:, c0 + 16:c0 + 17]:
                          e.tensor_scalar(out=o, in0=s, scalar1=m_, scalar2=r, op0=ALU.subtract, op1=ALU.mult),
                          waits=w, inc=s_dve)
            add("gpsimd", lambda e, s=xn2[:, b * D:(b + 1) * D], i_=i:
                e.dma_start(out=xnb_h[i_ * 128:(i_ + 1) * 128, :], in_=s),
                waits=[(s_dve, v_xn[i])], inc=s_xnw, k=16)
            if i % 4 == 2 and i >= 6:
                emit_tblock((i - 6) // 4)
        emit_tblock(7)
        emit_qk_cast(1)
        XNT_ALL = (s_xt, s_xt.n)

        add("scalar", lambda e: e.preload_act_table(AF.Exp))
        v_gexp = [0] * NT
        v_sevac = [0] * NT
        v_sdma = [0] * NT
        v_strp = [0] * NT
        for i in range(NT):
            b = i % 2
            w = [(s_xt, v_xt[i]), (s_gt, 16)]
            if i >= 1:
                w.append((s_act, v_gexp[i - 1]))
            v_gmm = 0
            for d in range(ND):
                v_gmm = add("tensor", lambda e, lt=xnT[:, d * T + i * 128: d * T + (i + 1) * 128],
                            r=gate_sb[:, d * E:(d + 1) * E], st=(d == 0), sp=(d == ND - 1):
                            e.matmul(psG[:, 0:E], lhsT=lt, rhs=r, start=st, stop=sp),
                            waits=(w if d == 0 else ()), inc=(s_pe if d == ND - 1 else None))
            gc = 4 * i
            v_ngmx = add("vector", lambda e, o=gsc[:, gc:gc + 1]: e.reduce_max(o, psG[:, 0:E], axis=AX.X, negate=True),
                         waits=[(s_pe, v_gmm)], inc=s_dve)
            v_gexp[i] = add("scalar", lambda e, o=gp2[:, b * 8:(b + 1) * 8], bi=gsc[:, gc:gc + 1]:
                            e.activation(out=o, in_=psG[:, 0:E], func=AF.Exp, bias=bi),
                            waits=[(s_dve, v_ngmx)], inc=s_act)
            v_den = add("vector", lambda e, o=gsc[:, gc + 1:gc + 2], s=gp2[:, b * 8:(b + 1) * 8]:
                        e.reduce_sum(o, s, axis=AX.X), waits=[(s_act, v_gexp[i])], inc=s_dve)
            v_rden = add("vector", lambda e, o=gsc[:, gc + 2:gc + 3], s=gsc[:, gc + 1:gc + 2]: e.reciprocal(o, s),
                         waits=[(s_dve, v_den)], inc=s_dve)
            v_probs = add("vector", lambda e, o=gp2[:, b * 8:(b + 1) * 8], s=gp2[:, b * 8:(b + 1) * 8], r=gsc[:, gc + 2:gc + 3]:
                          e.tensor_scalar(out=o, in0=s, scalar1=r, scalar2=None, op0=ALU.mult),
                          waits=[(s_dve, v_rden)], inc=s_dve)
            v_t8 = add("vector", lambda e, o=t82[:, b * 8:(b + 1) * 8], s=gp2[:, b * 8:(b + 1) * 8]: e.max(out=o, in_=s),
                       waits=[(s_dve, v_probs)], inc=s_dve)
            w = [(s_dve, v_t8)]
            if i >= 2:
                w.append((s_pe, v_strp[i - 2]))
            v_sc = add("vector", lambda e, o=sc2[:, b * 8:(b + 1) * 8], s=gp2[:, b * 8:(b + 1) * 8], m2=t82[:, b * 8 + 1:b * 8 + 2]:
                       e.scalar_tensor_tensor(out=o, in0=s, scalar=m2, in1=s, op0=ALU.is_ge, op1=ALU.mult),
                       waits=w, inc=s_dve)
            w = [(s_dve, v_sc), CST]
            if i >= 1:
                w.append((s_dve, v_sevac[i - 1]))
            v_strp[i] = add("tensor", lambda e, s=sc2[:, b * 8:(b + 1) * 8]: e.transpose(psS[0:8, 0:128], s, idf_sb[:, :]),
                            waits=w, inc=s_pe)
            w = [(s_pe, v_strp[i])]
            if i >= 2:
                w.append((s_sf[b], v_sdma[i - 2]))
            v_sevac[i] = add("vector", lambda e, o=sst2[0:8, b * 128:(b + 1) * 128]: e.tensor_copy(o, psS[0:8, 0:128]),
                             waits=w, inc=s_dve)
            v_sdma[i] = add("sync", lambda e, s=sst2[0:8, b * 128:(b + 1) * 128], i_=i:
                            e.dma_start(out=sfm_h[:, i_ * 128:(i_ + 1) * 128], in_=s),
                            waits=[(s_dve, v_sevac[i])], inc=s_sf[b], k=16)
        SFM_ALL = [(s_sf[0], s_sf[0].n), (s_sf[1], s_sf[1].n)]

        v_vevac = [0] * NT
        v_vmm = [0] * NT
        for i in range(NT):
            b = i % 2
            w0 = [(s_wv, 16), (s_xt, v_xt[i])]
            if i >= 2:
                w0.append((s_dve, v_vevac[i - 2]))
            for half in range(2):
                for d in range(ND):
                    v_vmm[i] = add("tensor", lambda e, o=psV[b][:, half * 512:(half + 1) * 512],
                                   lt=xnT[:, d * T + i * 128: d * T + (i + 1) * 128],
                                   r=wvb[:, d * 1024 + half * 512: d * 1024 + (half + 1) * 512],
                                   st=(d == 0), sp=(d == ND - 1):
                                   e.matmul(o, lhsT=lt, rhs=r, start=st, stop=sp),
                                   waits=(w0 if (half == 0 and d == 0) else ()),
                                   inc=(s_pe if (half == 1 and d == ND - 1) else None))
            va = v_aug[:, i * 1040:(i + 1) * 1040].rearrange("p (h c) -> p h c", c=65)
            add("vector", lambda e, o=va[:, :, 64:65]: e.memset(o, 1.0))
            v_vevac[i] = add("vector", lambda e, o=va[:, :, 0:64],
                             s=psV[b][:, :].rearrange("p (h k) -> p h k", k=64):
                             e.tensor_copy(o, s), waits=[(s_pe, v_vmm[i])], inc=s_dve)
        v_vaug_done = s_dve.n

        barrier()
        p012.close()

        p3 = ExitStack()
        qT_sb = p3.enter_context(nc.sbuf_tensor("qT_sb", [128, T], BF16))
        kT_sb = p3.enter_context(nc.sbuf_tensor("kT_sb", [128, T], BF16))
        PT2 = p3.enter_context(nc.sbuf_tensor("PT2", [128, 3 * 1024], BF16))
        R2 = p3.enter_context(nc.sbuf_tensor("R2", [64, 2 * 512], F32))
        pv_sb = p3.enter_context(nc.sbuf_tensor("pv_sb", [65, 4 * 512], F32))
        recl_sb = p3.enter_context(nc.sbuf_tensor("recl_sb", [1, 4 * 512], F32))
        ao2 = p3.enter_context(nc.sbuf_tensor("ao2", [64, 2 * 512], BF16))

        def emit_bounces():
            for e_ in range(E):
                add("gpsimd", lambda e, ee=e_: e.dma_start(out=w13b_h[ee], in_=w13_h[ee]), inc=s_bnc, k=16)
                add("gpsimd", lambda e, ee=e_: e.dma_start(out=w2b_h[ee], in_=w2_h[ee]), inc=s_bnc, k=16)
        BNC = (s_bnc, 16 * 2 * E)

        v_qk_last_read = 0
        v_mult_last = 0
        v_cp_last = 0
        v_ao_dma = {}
        v_multR = {}
        v_tail_last = {}
        v_r1_by_pvb = {}
        last_exp_on_buf = {}
        last_pv_on_buf = {}

        for c in range(NH // 2):
            gA, gB = 2 * c, 2 * c + 1
            if 1 <= c < NH // 2 - 1:
                emit_qk_cast(c + 1, waits=[(s_pe, v_qk_last_read)])
                if c == 1:
                    emit_bounces()
            v_wk = v_wk_pair[c]
            bqk = (c % 2) * 1024

            with nc.psum_tensor("psQ0_%d" % c, [128, 512], F32) as psQ0, \
                 nc.psum_tensor("psQ1_%d" % c, [128, 512], F32) as psQ1:
                psQ = [psQ0, psQ1]
                v_qevac = {}
                for (which, wsb, src, scl) in (("q", qT_sb, wqp, 0.125), ("k", kT_sb, wkp, 1.0)):
                    for tb in range(NB):
                        bq = tb % 2
                        w = [(s_qk, v_wk), XNT_ALL]
                        if (which, bq) in v_qevac:
                            w.append((s_act, v_qevac[(which, bq)]))
                        elif which == "k" and ("q", bq) in v_qevac:
                            w.append((s_act, v_qevac[("q", bq)]))
                        if c > 0 and which == "q" and tb < 2:
                            w.append((s_dve, v_mult_last))
                        vm = 0
                        for d in range(ND):
                            vm = add("tensor", lambda e, o=psQ[bq][:, :], lt=src[:, bqk + d * 128:bqk + (d + 1) * 128],
                                     r=xnT[:, d * T + tb * 512: d * T + (tb + 1) * 512],
                                     st=(d == 0), sp=(d == ND - 1):
                                     e.matmul(o, lhsT=lt, rhs=r, start=st, stop=sp),
                                     waits=(w if d == 0 else ()), inc=(s_pe if d == ND - 1 else None))
                        v_qevac[(which, bq)] = add("scalar", lambda e, o=wsb[:, tb * 512:(tb + 1) * 512], s=psQ[bq][:, :], sc=scl:
                                                   e.activation(out=o, in_=s, func=AF.Copy, scale=sc),
                                                   waits=[(s_pe, vm)], inc=s_act)
                v_qk_ready = max(v_qevac.values())

            with nc.psum_tensor("psST0_%d" % c, [128, 1024], F32) as psST0, \
                 nc.psum_tensor("psST1_%d" % c, [128, 1024], F32) as psST1, \
                 nc.psum_tensor("psST2_%d" % c, [128, 1024], F32) as psST2, \
                 nc.psum_tensor("psPVA_%d" % c, [65, 512], F32) as psPVA, \
                 nc.psum_tensor("psPVB_%d" % c, [65, 512], F32) as psPVB:
                psST = [psST0, psST1, psST2]
                for j in range(NB):
                    ks = 4 * j + 4
                    v_st = [0] * ks
                    v_exp = [0] * ks
                    v_gate_pv = [0] * ks
                    v_pv = [0] * ks

                    def emit_st(k):
                        bs = k % 3
                        w = [] if (j > 0 or k >= 3) else [(s_act, v_qk_ready), CST]
                        if bs in last_exp_on_buf:
                            w.append((s_act, last_exp_on_buf[bs]))
                        diag = k >= 4 * j
                        r_ = k - 4 * j
                        add("tensor", lambda e, o=psST[bs][:, 0:512], lt=kT_sb[0:64, k * 128:(k + 1) * 128],
                            r=qT_sb[0:64, j * 512:(j + 1) * 512], sp=not diag:
                            e.matmul(o, lhsT=lt, rhs=r, start=True, stop=sp, tile_position=(0, 0)), waits=w)
                        v_stA = add("tensor", lambda e, o=psST[bs][:, 512:1024], lt=kT_sb[64:128, k * 128:(k + 1) * 128],
                                    r=qT_sb[64:128, j * 512:(j + 1) * 512], sp=not diag:
                                    e.matmul(o, lhsT=lt, rhs=r, start=True, stop=sp, tile_position=(64, 0)),
                                    inc=(None if diag else s_pe))
                        if diag:
                            add("tensor", lambda e, o=psST[bs][:, 0:512], m=cm_sb[:, r_ * 512:(r_ + 1) * 512]:
                                e.matmul(o, lhsT=idb_sb[:, :], rhs=m, start=False, stop=True))
                            v_stA = add("tensor", lambda e, o=psST[bs][:, 512:1024], m=cm_sb[:, r_ * 512:(r_ + 1) * 512]:
                                        e.matmul(o, lhsT=idb_sb[:, :], rhs=m, start=False, stop=True), inc=s_pe)
                        v_st[k] = v_stA

                    for kk in range(min(3, ks)):
                        emit_st(kk)
                    for k in range(ks):
                        bs = k % 3
                        w = [(s_pe, v_st[k])]
                        if bs in last_pv_on_buf:
                            w.append((s_pe, last_pv_on_buf[bs]))
                        v_exp[k] = add("scalar", lambda e, o=PT2[:, bs * 1024:(bs + 1) * 1024], s=psST[bs][:, :]:
                                       e.activation(out=o, in_=s, func=AF.Exp), waits=w, inc=s_act)
                        last_exp_on_buf[bs] = v_exp[k]
                        w = [(s_act, v_exp[k])]
                        if k == 0:
                            w.append((s_dve, v_cp_last))
                            w.append((s_dve, v_vaug_done))
                        add("tensor", lambda e, o=psPVA[:, :], lt=v_aug[:, k * 1040 + 65 * gA: k * 1040 + 65 * gA + 65],
                            r=PT2[:, bs * 1024: bs * 1024 + 512], st=(k == 0), sp=(k == ks - 1):
                            e.matmul(o, lhsT=lt, rhs=r, start=st, stop=sp), waits=w)
                        v_pv[k] = add("tensor", lambda e, o=psPVB[:, :], lt=v_aug[:, k * 1040 + 65 * gB: k * 1040 + 65 * gB + 65],
                                      r=PT2[:, bs * 1024 + 512: bs * 1024 + 1024], st=(k == 0), sp=(k == ks - 1):
                                      e.matmul(o, lhsT=lt, rhs=r, start=st, stop=sp), inc=s_pe)
                        last_pv_on_buf[bs] = v_pv[k]
                        if k + 3 < ks:
                            emit_st(k + 3)

                    v_cp = {}
                    for (g_, psPV, aob) in ((gA, psPVA, 0), (gB, psPVB, 1)):
                        pvb = (j % 2) * 2 + aob
                        w = [(s_pe, v_pv[ks - 1])]
                        if pvb in v_tail_last:
                            w.append((s_dve, v_tail_last[pvb]))
                        v_cp[aob] = add("vector", lambda e, o=pv_sb[:, pvb * 512:(pvb + 1) * 512], p=psPV:
                                        e.tensor_copy(o, p[:, :]), waits=w, inc=s_dve)
                    v_cp_last = max(v_cp.values())
                    for (g_, psPV, aob) in ((gA, psPVA, 0), (gB, psPVB, 1)):
                        pvb = (j % 2) * 2 + aob
                        w = [(s_dve, v_cp[aob])]
                        if pvb in v_r1_by_pvb:
                            w.append((s_rb[aob], v_r1_by_pvb[pvb]))
                        v_rc = add("vector", lambda e, o=recl_sb[0:1, pvb * 512:(pvb + 1) * 512],
                                   p=pv_sb[64:65, pvb * 512:(pvb + 1) * 512]:
                                   e.reciprocal(o, p), waits=w, inc=s_dve)
                        w = [(s_dve, v_rc)]
                        if aob in v_multR:
                            w.append((s_dve, v_multR[aob]))
                        v_r1 = add("sync", lambda e, r=recl_sb[0:1, pvb * 512:(pvb + 1) * 512], ab=aob:
                                   e.dma_start(out=rl_h[ab:ab + 1, :], in_=r), waits=w, inc=s_rb[aob], k=16)
                        v_r1_by_pvb[pvb] = v_r1
                        v_rdma = add("sync", lambda e, o=R2[:, aob * 512:(aob + 1) * 512], ab=aob:
                                     e.dma_start(out=o, in_=rl_h[ab, :].partition_broadcast(64)),
                                     waits=[(s_rb[aob], v_r1)], inc=s_rb[aob], k=16)
                        w = [(s_rb[aob], v_rdma)]
                        if aob in v_ao_dma:
                            w.append((s_ao[aob], v_ao_dma[aob]))
                        v_multR[aob] = add("vector", lambda e, o=ao2[:, aob * 512:(aob + 1) * 512],
                                           p=pv_sb[0:64, pvb * 512:(pvb + 1) * 512], r=R2[:, aob * 512:(aob + 1) * 512]:
                                           e.tensor_mul(o, p, r), waits=w, inc=s_dve)
                        v_mult_last = v_multR[aob]
                        v_tail_last[pvb] = v_multR[aob]
                        v_ao_dma[aob] = add("sync", lambda e, s=ao2[:, aob * 512:(aob + 1) * 512], g2=g_, j_=j:
                                            e.dma_start(out=atT_h[g2 * 64:(g2 + 1) * 64, j_ * 512:(j_ + 1) * 512], in_=s),
                                            waits=[(s_dve, v_multR[aob])], inc=s_ao[aob], k=16)
            v_qk_last_read = s_pe.n
        AO_ALL = [(s_ao[0], s_ao[0].n), (s_ao[1], s_ao[1].n)]

        barrier()
        p3.close()
        v_aug_cm.__exit__(None, None, None)
        wkp_cm.__exit__(None, None, None)
        wqp_cm.__exit__(None, None, None)

        p5 = ExitStack()
        w13t = p5.enter_context(nc.sbuf_tensor("w13t", [128, 2 * ND * 1024], BF16))
        g_sb = p5.enter_context(nc.sbuf_tensor("g_sb", [128, E * 4 * 512], BF16))
        wob = p5.enter_context(nc.sbuf_tensor("wob", [128, ND * 1024], BF16))
        S_bc = p5.enter_context(nc.sbuf_tensor("S_bc", [128, 3 * 512], F32))
        xs_sb = p5.enter_context(nc.sbuf_tensor("xs_sb", [128, 2 * ND * 512], BF16))
        sa_sb = p5.enter_context(nc.sbuf_tensor("sa_sb", [128, 2 * 512], F32))
        w2t = p5.enter_context(nc.sbuf_tensor("w2t", [128, 2 * 2048], BF16))
        at5 = p5.enter_context(nc.sbuf_tensor("at5", [128, 2 * 1024], BF16))
        x5 = p5.enter_context(nc.sbuf_tensor("x5", [128, 4 * 512], F32))
        o5 = p5.enter_context(nc.sbuf_tensor("o5", [128, 4 * 512], F32))

        psU1 = [p5.enter_context(nc.psum_tensor("psU1%d" % s, [128, 512], F32)) for s in range(2)]
        psU2 = [p5.enter_context(nc.psum_tensor("psU2%d" % s, [128, 512], F32)) for s in range(2)]
        psY = [p5.enter_context(nc.psum_tensor("psY%d" % s, [128, 512], F32)) for s in range(4)]

        add("scalar", lambda e: e.preload_act_table(AF.Silu))
        add("gpsimd", lambda e: e.dma_start(out=wob[:, :].rearrange("p (dc o) -> p dc o", o=1024), in_=woT_h[:, :].rearrange("(dc p) o -> p dc o", p=128)),
            inc=s_wo, k=16)

        v_w13t = {}
        v_u_last_on = {}
        v_sbc = {}
        v_xs = {}
        v_xs_reader = {}
        v_sbc_reader = {}
        v_g = {}
        v_silu = {}
        v_gdone_tb = 0
        v_oadd = {}
        v_outdma = {}
        v_w2dma = {}
        v_atdma = {}
        v_x5dma = {}
        v_w2_last_read = 0

        for tb in range(NB):
            bw = 0
            if tb == 0:
                v_w13t[0] = add("sync", lambda e: e.dma_start(
                    out=w13t[:, 0:8192].rearrange("p (dc hh) -> p dc hh", hh=1024),
                    in_=w13b_h[0].rearrange("(dc p) hh -> p dc hh", p=128)), waits=[BNC], inc=s_w13[0], k=16)

            def emit_sbc(ge2):
                ee, tb2 = ge2 % E, ge2 // E
                bs3 = ge2 % 3
                w = list(SFM_ALL)
                if bs3 in v_sbc_reader:
                    w.append((s_dve, v_sbc_reader[bs3]))
                v_sbc[bs3] = add("sync", lambda e, e2=ee, t0=tb2 * 512, o=S_bc[:, bs3 * 512:(bs3 + 1) * 512]:
                                 e.dma_start(out=o, in_=sfm_h[e2, t0:t0 + 512].partition_broadcast(128)),
                                 waits=w, inc=s_sbc[bs3], k=16)

            def emit_xs(ge2):
                ee, tb2 = ge2 % E, ge2 // E
                bs3 = ge2 % 3
                be = ge2 % 2
                vxs = 0
                for d in range(ND):
                    w = [(s_sbc[bs3], v_sbc[bs3]), XNT_ALL]
                    if d == 0 and be in v_xs_reader:
                        w.append((s_pe, v_xs_reader[be]))
                    vxs = add("vector", lambda e, o=xs_sb[:, (be * ND + d) * 512:(be * ND + d + 1) * 512],
                              xn_=xnT[:, d * T + tb2 * 512: d * T + (tb2 + 1) * 512],
                              sb_=S_bc[:, bs3 * 512:(bs3 + 1) * 512]:
                              e.tensor_mul(o, xn_, sb_), waits=(w if d == 0 else ()), inc=s_dve)
                v_xs[be] = vxs
                v_sbc_reader[bs3] = vxs

            if tb == 0:
                emit_sbc(0)
                emit_sbc(1)
                emit_xs(0)
            for e_ in range(E):
                bu = (e_ + bw) % 2
                bx = e_ % 2
                ge = tb * E + e_
                if ge + 2 < NB * E:
                    emit_sbc(ge + 2)
                if ge + 1 < NB * E:
                    emit_xs(ge + 1)
                ep = e_ + 1 if e_ + 1 < E else (0 if tb + 1 < NB else None)
                if ep is not None:
                    nb_ = ep % 2
                    w = [BNC]
                    if nb_ in v_u_last_on:
                        w.append((s_pe, v_u_last_on[nb_]))
                    v_w13t[nb_] = add("sync", lambda e, ee=ep, nb2=nb_: e.dma_start(
                        out=w13t[:, nb2 * 8192: nb2 * 8192 + 8192].rearrange("p (dc hh) -> p dc hh", hh=1024),
                        in_=w13b_h[ee].rearrange("(dc p) hh -> p dc hh", p=128)), waits=w, inc=s_w13[nb_], k=16)
                for h in range(4):
                    bh = h % 2
                    wu = [(s_w13[bu], v_w13t[bu]), (s_dve, v_xs[bx])]
                    if bh in v_silu:
                        wu.append((s_act, v_silu[bh]))
                    vu1 = 0
                    for d in range(ND):
                        vu1 = add("tensor", lambda e, o=psU1[bh][:, :], lt=w13t[:, bu * 8192 + d * 1024 + h * 128: bu * 8192 + d * 1024 + (h + 1) * 128],
                                  r=xs_sb[:, (bx * ND + d) * 512:(bx * ND + d + 1) * 512], st=(d == 0), sp=(d == ND - 1):
                                  e.matmul(o, lhsT=lt, rhs=r, start=st, stop=sp),
                                  waits=(wu if d == 0 else ()), inc=(s_pe if d == ND - 1 else None))
                    wu = []
                    if bh in v_g:
                        wu.append((s_dve, v_g[bh]))
                    vu3 = 0
                    for d in range(ND):
                        vu3 = add("tensor", lambda e, o=psU2[bh][:, :], lt=w13t[:, bu * 8192 + d * 1024 + 512 + h * 128: bu * 8192 + d * 1024 + 512 + (h + 1) * 128],
                                  r=xs_sb[:, (bx * ND + d) * 512:(bx * ND + d + 1) * 512], st=(d == 0), sp=(d == ND - 1):
                                  e.matmul(o, lhsT=lt, rhs=r, start=st, stop=sp),
                                  waits=(wu if d == 0 else ()), inc=(s_pe if d == ND - 1 else None))
                    v_u_last_on[bu] = vu3
                    v_xs_reader[bx] = vu3
                    w = [(s_pe, vu1)]
                    if bh in v_g:
                        w.append((s_dve, v_g[bh]))
                    v_silu[bh] = add("scalar", lambda e, o=sa_sb[:, bh * 512:(bh + 1) * 512], p=psU1[bh]:
                                     e.activation(out=o, in_=p[:, :], func=AF.Silu), waits=w, inc=s_act)
                    slot = (e_ * 4 + h)
                    w = [(s_act, v_silu[bh]), (s_pe, vu3)]
                    if v_w2_last_read and tb > 0:
                        w.append((s_pe, v_w2_last_read))
                    v_g[bh] = add("vector", lambda e, o=g_sb[:, slot * 512:(slot + 1) * 512],
                                  sa=sa_sb[:, bh * 512:(bh + 1) * 512], p=psU2[bh]:
                                  e.tensor_mul(o, sa, p[:, :]), waits=w, inc=s_dve)
            v_gdone_tb = s_dve.n
            if tb == 0 and "g0" in dbg_h:
                add("sync", lambda e: e.dma_start(out=dbg_h["g0"][:, :], in_=g_sb[:, :]),
                    waits=[(s_dve, v_gdone_tb)], inc=s_out, k=16)

            for half in range(2):
                vy_stop = [0] * 4
                for dq in range(4):
                    bat = dq % 2
                    w = list(AO_ALL)
                    if bat in v_atdma:
                        w.append((s_pe, v_atdma[bat][1]))
                    if dq == 0:
                        for s4 in range(4):
                            if s4 in v_oadd:
                                w.append((s_dve, v_oadd[s4]))
                    vdma = add("sync", lambda e, d0=dq * 2, o=at5[:, bat * 1024:(bat + 1) * 1024].rearrange("p (dc c) -> p dc c", c=512), t0=tb * 512:
                               e.dma_start(out=o, in_=atT_h[d0 * 128:(d0 + 2) * 128, t0:t0 + 512].rearrange("(dc p) c -> p dc c", p=128)),
                               waits=w, inc=s_at[bat], k=16)
                    vmm = 0
                    for di in range(2):
                        d = dq * 2 + di
                        for sub in range(4):
                            w = [(s_at[bat], vdma), (s_wo, 16)]
                            if d == 0 and sub in v_oadd:
                                w.append((s_dve, v_oadd[sub]))
                            vmm = add("tensor", lambda e, o=psY[sub][:, :],
                                      lt=at5[:, bat * 1024 + di * 512 + sub * 128: bat * 1024 + di * 512 + (sub + 1) * 128],
                                      r=wob[:, d * 1024 + half * 512: d * 1024 + (half + 1) * 512],
                                      st=(d == 0), sp=False:
                                      e.matmul(o, lhsT=lt, rhs=r, start=st, stop=sp),
                                      waits=(w if sub == 0 else ()), inc=(s_pe if (sub == 3 and di == 1) else None))
                    v_atdma[bat] = (vdma, vmm)
                for e_ in range(E):
                    bwt = e_ % 2
                    w = [BNC]
                    if bwt in v_w2dma:
                        w.append((s_pe, v_w2dma[bwt][1]))
                    vdma = add("sync", lambda e, ee=e_, hf=half, o=w2t[:, bwt * 2048:(bwt + 1) * 2048].rearrange("p (hh c) -> p hh c", c=512):
                               e.dma_start(out=o, in_=w2b_h[ee][:, hf * 512:(hf + 1) * 512].rearrange("(hh p) c -> p hh c", p=128)),
                               waits=w, inc=s_w2[bwt], k=16)
                    vmm = 0
                    for h in range(4):
                        slot = e_ * 4 + h
                        for sub in range(4):
                            w = [(s_w2[bwt], vdma), (s_dve, v_gdone_tb if (e_ == 0 and h == 0 and sub == 0) else 0)]
                            if e_ == E - 1 and h == 3:
                                vy_stop[sub] = 0
                            vmm = add("tensor", lambda e, o=psY[sub][:, :],
                                      lt=g_sb[:, slot * 512 + sub * 128: slot * 512 + (sub + 1) * 128],
                                      r=w2t[:, bwt * 2048 + h * 512: bwt * 2048 + (h + 1) * 512],
                                      st=False, sp=(e_ == E - 1 and h == 3):
                                      e.matmul(o, lhsT=lt, rhs=r, start=st, stop=sp),
                                      waits=(w if sub == 0 else ()), inc=(s_pe if (h == 3) else None))
                            if e_ == E - 1 and h == 3:
                                vy_stop[sub] = vmm
                    v_w2dma[bwt] = (vdma, vmm)
                v_w2_last_read = s_pe.n
                if tb == 0 and half == 0 and "y0" in dbg_h:
                    y0_sb = p5.enter_context(nc.sbuf_tensor("y0_sb", [128, 4 * 512], F32))
                    vy0 = 0
                    for s4 in range(4):
                        vy0 = add("vector", lambda e, o=y0_sb[:, s4 * 512:(s4 + 1) * 512], p_=psY[s4]:
                                  e.tensor_copy(o, p_[:, :]), waits=[(s_pe, vy_stop[s4])], inc=s_dve)
                    add("sync", lambda e: e.dma_start(out=dbg_h["y0"][:, :], in_=y0_sb[:, :]),
                        waits=[(s_dve, vy0)], inc=s_out, k=16)
                vx4 = [0] * 4
                for sub in range(4):
                    row0 = tb * 512 + sub * 128
                    w = []
                    if sub in v_x5dma:
                        w.append((s_dve, v_x5dma[sub]))
                    vx4[sub] = add("sync", lambda e, o=x5[:, sub * 512:(sub + 1) * 512], r0=row0, hf=half:
                                   e.dma_start(out=o, in_=x_h[r0:r0 + 128, hf * 512:(hf + 1) * 512]),
                                   waits=w, inc=s_x5[sub], k=16)
                for sub in range(4):
                    row0 = tb * 512 + sub * 128
                    w = [(s_pe, vy_stop[sub]), (s_x5[sub], vx4[sub])]
                    if sub in v_outdma:
                        w.append((s_o[sub], v_outdma[sub]))
                    v_oadd[sub] = add("vector", lambda e, o=o5[:, sub * 512:(sub + 1) * 512], s=sub, x_=x5[:, sub * 512:(sub + 1) * 512]:
                                      e.tensor_add(o, psY[s][:, :], x_), waits=w, inc=s_dve)
                    v_x5dma[sub] = v_oadd[sub]
                    v_outdma[sub] = add("sync", lambda e, s_=o5[:, sub * 512:(sub + 1) * 512], r0=row0, hf=half:
                                        e.dma_start(out=out_h[r0:r0 + 128, hf * 512:(hf + 1) * 512], in_=s_),
                                        waits=[(s_dve, v_oadd[sub])], inc=s_o[sub], k=16)

        if "xnT" in dbg_h:
            for cch in range(ND):
                add("sync", lambda e, c2=cch: e.dma_start(out=dbg_h["xnT"][c2 * 128:(c2 + 1) * 128, :],
                    in_=xnT[:, c2 * T:(c2 + 1) * T]), waits=[XNT_ALL], inc=s_out, k=16)
        if "scale" in dbg_h:
            add("sync", lambda e: e.dma_start(out=dbg_h["scale"][:, :], in_=sfm_h[:, :]),
                waits=SFM_ALL, inc=s_out, k=16)
        if "attnT" in dbg_h:
            add("sync", lambda e: e.dma_start(out=dbg_h["attnT"][:, :], in_=atT_h[:, :]),
                waits=AO_ALL, inc=s_out, k=16)

        add("sync", lambda e: e.nop(),
            waits=[(so, so.n) for so in s_o] + [(s_out, s_out.n)])

        block = ctx.enter_context(nc.Block())
        block.sync(lambda eng: [f(eng) for f in P["sync"]])
        block.gpsimd(lambda eng: [f(eng) for f in P["gpsimd"]])
        block.scalar(lambda eng: [f(eng) for f in P["scalar"]])
        block.vector(lambda eng: [f(eng) for f in P["vector"]])
        block.tensor(lambda eng: [f(eng) for f in P["tensor"]])

        p5.close()
    return nc


def _prep_inputs(L0_attn_x, L0_attn_Wq, L0_attn_Wk, L0_attn_Wv, L0_attn_Wo,
                 L0_ffn_gate_w, L0_ffn_w1, L0_ffn_w2, L0_ffn_w3):
    f32 = np.float32
    x = np.ascontiguousarray(np.asarray(L0_attn_x, f32))
    wqT = np.ascontiguousarray(np.asarray(L0_attn_Wq, f32).T)
    wkT = np.ascontiguousarray(np.asarray(L0_attn_Wk, f32).T)
    wvT = np.ascontiguousarray(np.asarray(L0_attn_Wv, f32).T)
    woT = np.ascontiguousarray(np.asarray(L0_attn_Wo, f32).T)
    gw = np.ascontiguousarray(np.asarray(L0_ffn_gate_w, f32))
    w13 = np.ascontiguousarray(np.concatenate(
        [np.asarray(L0_ffn_w1, f32), np.asarray(L0_ffn_w3, f32)], axis=2))
    w2 = np.ascontiguousarray(np.asarray(L0_ffn_w2, f32))

    idf = np.eye(128, dtype=f32)
    idb = np.eye(128, dtype=ml_dtypes.bfloat16)
    i_idx = np.arange(128)[:, None]
    j_idx = np.arange(512)[None, :]
    cm = np.zeros((128, 4 * 512), dtype=np.float32)
    for r in range(4):
        cm[:, r * 512:(r + 1) * 512] = np.where((128 * r + i_idx) > j_idx, -30000.0, 0.0)
    cm = cm.astype(ml_dtypes.bfloat16)
    ones = np.ones((1, 128), f32)

    shared = dict(wqT=wqT, wkT=wkT, wvT=wvT, woT=woT, gw=gw, w13=w13, w2=w2,
                  idf=idf, idb=idb, cmask=cm, ones=ones)
    maps = []
    for i in range(N_CORES):
        m = dict(shared)
        m["x"] = np.ascontiguousarray(x[i])
        maps.append(m)
    return maps


def kernel(L0_attn_x, L0_attn_Wq, L0_attn_Wk, L0_attn_Wv, L0_attn_Wo,
           L0_ffn_gate_w, L0_ffn_w1, L0_ffn_w2, L0_ffn_w3):
    maps = _prep_inputs(L0_attn_x, L0_attn_Wq, L0_attn_Wk, L0_attn_Wv, L0_attn_Wo,
                        L0_ffn_gate_w, L0_ffn_w1, L0_ffn_w2, L0_ffn_w3)
    nc = build()
    res = run_bass_kernel_spmd(nc, maps, core_ids=list(range(N_CORES)))
    return np.stack([np.asarray(res.results[i]["out"]) for i in range(N_CORES)], axis=0)


# version 34 (final)
# speedup vs baseline: 1.0431x; 1.0431x over previous
import numpy as np
import ml_dtypes
from contextlib import ExitStack

import concourse.bass as bass
import concourse.mybir as mybir
from concourse.bass_utils import run_bass_kernel_spmd

F32 = mybir.dt.float32
BF16 = mybir.dt.bfloat16
AF = mybir.ActivationFunctionType
ALU = mybir.AluOpType
AX = mybir.AxisListType

T, D, NH, DK, E, HF = 4096, 1024, 16, 64, 8, 512
NT = T // 128
ND = D // 128
NB = T // 512
EPS = 1e-5
N_CORES = 8


class Sem:
    __slots__ = ("h", "n")

    def __init__(self, h):
        self.h = h
        self.n = 0

    def inc(self, k=1):
        self.n += k
        return self.n


def build(dbg=()):
    nc = bass.Bass("TRN2", target_bir_lowering=False)

    x_h = nc.dram_tensor("x", [T, D], F32, kind="ExternalInput")
    wqT_h = nc.dram_tensor("wqT", [D, D], F32, kind="ExternalInput")
    wkT_h = nc.dram_tensor("wkT", [D, D], F32, kind="ExternalInput")
    wvT_h = nc.dram_tensor("wvT", [D, D], F32, kind="ExternalInput")
    woT_h = nc.dram_tensor("woT", [D, D], F32, kind="ExternalInput")
    gw_h = nc.dram_tensor("gw", [D, E], F32, kind="ExternalInput")
    w13_h = nc.dram_tensor("w13", [E, D, 2 * HF], F32, kind="ExternalInput")
    w2_h = nc.dram_tensor("w2", [E, HF, D], F32, kind="ExternalInput")
    idf_h = nc.dram_tensor("idf", [128, 128], F32, kind="ExternalInput")
    idb_h = nc.dram_tensor("idb", [128, 128], BF16, kind="ExternalInput")
    cm_h = nc.dram_tensor("cmask", [128, 4 * 512], BF16, kind="ExternalInput")
    ones_h = nc.dram_tensor("ones", [1, 128], F32, kind="ExternalInput")
    out_h = nc.dram_tensor("out", [T, D], F32, kind="ExternalOutput")

    sfm_h = nc.dram_tensor("sfm", [E, T], F32)
    xnb_h = nc.dram_tensor("xnb", [T, D], BF16)
    rl_h = nc.dram_tensor("rl", [2, 512], F32)
    atT_h = nc.dram_tensor("atT", [D, T], BF16)
    w13b_h = nc.dram_tensor("w13b", [E, D, 2 * HF], BF16)
    w2b_h = nc.dram_tensor("w2b", [E, HF, D], BF16)

    dbg_h = {}
    for name, shape, dt in [
        ("xnT", [D, T], BF16),
        ("scale", [E, T], F32),
        ("attnT", [D, T], BF16),
        ("g0", [128, E * 4 * 512], BF16),
        ("y0", [128, 4 * 512], F32),
    ]:
        if name in dbg:
            dbg_h[name] = nc.dram_tensor("dbg_" + name, shape, dt, kind="ExternalOutput")

    P = dict(sync=[], gpsimd=[], scalar=[], vector=[], tensor=[])

    def add(en, fn, waits=(), inc=None, k=1):
        ws = tuple((s.h, int(v)) for (s, v) in waits if v > 0)
        if inc is not None:
            val = inc.inc(k)
        else:
            val = None

        def g(eng, ws=ws, fn=fn, inc=inc, k=k):
            for h, v in ws:
                eng.wait_ge(h, v)
            r = fn(eng)
            if inc is not None:
                r.then_inc(inc.h, k)

        P[en].append(g)
        return val

    with ExitStack() as ctx:
        def sbuf(name, shape, dt):
            return ctx.enter_context(nc.sbuf_tensor(name, shape, dt))

        def mksem(name):
            return Sem(ctx.enter_context(nc.semaphore(name)))

        s_pe, s_act, s_dve = mksem("s_pe"), mksem("s_act"), mksem("s_dve")
        s_cst = mksem("s_cst")
        s_x = [mksem("s_x0"), mksem("s_x1")]
        s_xnw = mksem("s_xnw")
        s_xt = mksem("s_xt")
        s_sf = [mksem("s_sf0"), mksem("s_sf1")]
        s_gt, s_wv, s_wo = mksem("s_gt"), mksem("s_wv"), mksem("s_wo")
        s_qk = mksem("s_qk")
        s_ao = [mksem("s_ao0"), mksem("s_ao1")]
        s_rb = [mksem("s_rb0"), mksem("s_rb1")]
        s_sbc = [mksem("s_sbc%d" % i) for i in range(3)]
        s_w13 = [mksem("s_w13a"), mksem("s_w13b")]
        s_w2 = [mksem("s_w2a"), mksem("s_w2b")]
        s_at = [mksem("s_at0"), mksem("s_at1")]
        s_x5 = [mksem("s_x5%d" % i) for i in range(4)]
        s_o = [mksem("s_o%d" % i) for i in range(4)]
        s_bnc = mksem("s_bnc")
        s_out = mksem("s_out")
        all_sems = [s_pe, s_act, s_dve, s_cst, s_gt, s_wv, s_wo, s_qk, s_xnw, s_xt,
                    s_bnc, s_out] + s_x + s_sf + s_ao + s_w13 + s_w2 + s_at + s_x5 + s_o + s_rb + s_sbc

        def barrier():
            snap = [(s, s.n) for s in all_sems]
            for en in ("sync", "gpsimd", "scalar", "vector", "tensor"):
                add(en, lambda e: e.nop(), waits=snap)

        xnT = sbuf("xnT", [128, ND * T], BF16)
        idf_sb = sbuf("idf_sb", [128, 128], F32)
        idb_sb = sbuf("idb_sb", [128, 128], BF16)
        ones_sb = sbuf("ones_sb", [1, 128], F32)
        cm_sb = sbuf("cm_sb", [128, 2048], BF16)

        wqp_cm = nc.sbuf_tensor("wqp", [128, 2048], BF16)
        wqp = wqp_cm.__enter__()
        wkp_cm = nc.sbuf_tensor("wkp", [128, 2048], BF16)
        wkp = wkp_cm.__enter__()
        v_aug_cm = nc.sbuf_tensor("v_aug", [128, NT * 1040], BF16)
        v_aug = v_aug_cm.__enter__()
        add("sync", lambda e: e.dma_start(out=idf_sb[:, :], in_=idf_h[:, :]), inc=s_cst, k=16)
        add("sync", lambda e: e.dma_start(out=idb_sb[:, :], in_=idb_h[:, :]), inc=s_cst, k=16)
        add("sync", lambda e: e.dma_start(out=ones_sb[:, :], in_=ones_h[:, :]), inc=s_cst, k=16)
        add("sync", lambda e: e.dma_start(out=cm_sb[:, :], in_=cm_h[:, :]), inc=s_cst, k=16)
        CST = (s_cst, 64)

        p012 = ExitStack()
        xbuf = p012.enter_context(nc.sbuf_tensor("xbuf", [128, 4 * D], F32))
        xn2 = p012.enter_context(nc.sbuf_tensor("xn2", [128, 4 * D], BF16))
        stats = p012.enter_context(nc.sbuf_tensor("stats", [128, 18 * NT], F32))
        gp2 = p012.enter_context(nc.sbuf_tensor("gp2", [128, 2 * 8], F32))
        t82 = p012.enter_context(nc.sbuf_tensor("t82", [128, 2 * 8], F32))
        sc2 = p012.enter_context(nc.sbuf_tensor("sc2", [128, 2 * 8], F32))
        sst2 = p012.enter_context(nc.sbuf_tensor("sst2", [8, 2 * 128], F32))
        gsc = p012.enter_context(nc.sbuf_tensor("gsc", [128, 4 * NT], F32))
        gate_sb = p012.enter_context(nc.sbuf_tensor("gate_sb", [128, ND * E], BF16))
        wvb = p012.enter_context(nc.sbuf_tensor("wvb", [128, ND * 1024], BF16))

        psG0 = p012.enter_context(nc.psum_tensor("psG0", [128, 512], F32))
        psG1 = p012.enter_context(nc.psum_tensor("psG1", [128, 512], F32))
        psG = [psG0, psG1]
        psS = p012.enter_context(nc.psum_tensor("psS", [8, 512], F32))
        psV0 = p012.enter_context(nc.psum_tensor("psV0", [128, 1024], F32))
        psV1 = p012.enter_context(nc.psum_tensor("psV1", [128, 1024], F32))
        psV = [psV0, psV1]

        add("scalar", lambda e: e.preload_act_table(AF.Sqrt))

        v_x = [0] * NT
        v_xn = [0] * NT
        v_xt = [0] * NT

        add("gpsimd", lambda e: e.dma_start(out=gate_sb[:, :].rearrange("p (dc ee) -> p dc ee", ee=E), in_=gw_h[:, :].rearrange("(dc p) ee -> p dc ee", p=128)),
            inc=s_gt, k=16)
        add("gpsimd", lambda e: e.dma_start(out=wvb[:, :].rearrange("p (dc o) -> p dc o", o=1024), in_=wvT_h[:, :].rearrange("(dc p) o -> p dc o", p=128)),
            inc=s_wv, k=16)
        v_wk_pair = [0] * (NH // 2)

        def emit_qk_cast(cc, waits=()):
            bqk = (cc % 2) * 1024
            add("gpsimd", lambda e, cc=cc: e.dma_start(
                out=wqp[:, bqk:bqk + 1024].rearrange("p (dc o) -> p dc o", o=128),
                in_=wqT_h[:, cc * 128:(cc + 1) * 128].rearrange("(dc p) o -> p dc o", p=128)),
                waits=waits, inc=s_qk, k=16)
            v_wk_pair[cc] = add("gpsimd", lambda e, cc=cc: e.dma_start(
                out=wkp[:, bqk:bqk + 1024].rearrange("p (dc o) -> p dc o", o=128),
                in_=wkT_h[:, cc * 128:(cc + 1) * 128].rearrange("(dc p) o -> p dc o", p=128)),
                inc=s_qk, k=16)

        emit_qk_cast(0)

        def emit_tblock(t_):
            for d in range(ND):
                add("sync", lambda e, d_=d, t__=t_:
                    e.dma_start(out=xnT[:, d_ * T + t__ * 512: d_ * T + t__ * 512 + 512],
                                in_=xnb_h[t__ * 512:(t__ + 1) * 512, d_ * 128:(d_ + 1) * 128],
                                transpose=True),
                    waits=([(s_xnw, 16 * 4 * (t_ + 1))] if d == 0 else ()), inc=s_xt, k=16)
            for ii in range(4 * t_, 4 * t_ + 4):
                v_xt[ii] = s_xt.n

        v_std = [0] * NT

        def ln_tail(i):
            b = i % 4
            c0 = 18 * i
            v_rstd = add("vector", lambda e, o=stats[:, c0 + 16:c0 + 17], s=stats[:, c0 + 15:c0 + 16]: e.reciprocal(o, s),
                         waits=[(s_act, v_std[i])], inc=s_dve)
            w = [(s_dve, v_rstd)]
            if i >= 4:
                w.append((s_xnw, 16 * (i - 3)))
            v_xn[i] = add("vector", lambda e, o=xn2[:, b * D:(b + 1) * D], s=xbuf[:, b * D:(b + 1) * D],
                          m_=stats[:, c0 + 12:c0 + 13], r=stats[:, c0 + 16:c0 + 17]:
                          e.tensor_scalar(out=o, in0=s, scalar1=m_, scalar2=r, op0=ALU.subtract, op1=ALU.mult),
                          waits=w, inc=s_dve)
            add("gpsimd", lambda e, s=xn2[:, b * D:(b + 1) * D], i_=i:
                e.dma_start(out=xnb_h[i_ * 128:(i_ + 1) * 128, :], in_=s),
                waits=[(s_dve, v_xn[i])], inc=s_xnw, k=16)

        for i in range(NT):
            b = i % 4
            xs = xbuf[:, b * D:(b + 1) * D]
            w = []
            if i >= 4:
                w = [(s_dve, v_xn[i - 4])]
            v_x[i] = add("sync", lambda e, o=xs, i_=i: e.dma_start(out=o, in_=x_h[i_ * 128:(i_ + 1) * 128, :]),
                         waits=w, inc=s_x[i % 2], k=16)
            c0 = 18 * i
            add("vector", lambda e, o=stats[:, c0:c0 + 6], s=xs: e.bn_stats(o, s[:, 0:512]),
                waits=[(s_x[i % 2], v_x[i])], inc=s_dve)
            v_bn = add("vector", lambda e, o=stats[:, c0 + 6:c0 + 12], s=xs: e.bn_stats(o, s[:, 512:1024]),
                       inc=s_dve)
            v_ag = add("vector", lambda e, o=stats[:, c0 + 12:c0 + 14], s=stats[:, c0:c0 + 12]: e.bn_aggr(o, s),
                       waits=[(s_dve, v_bn)], inc=s_dve)
            v_var = add("vector", lambda e, o=stats[:, c0 + 14:c0 + 15], s=stats[:, c0 + 13:c0 + 14]:
                        e.tensor_scalar(out=o, in0=s, scalar1=EPS, scalar2=None, op0=ALU.add),
                        waits=[(s_dve, v_ag)], inc=s_dve)
            v_std[i] = add("scalar", lambda e, o=stats[:, c0 + 15:c0 + 16], s=stats[:, c0 + 14:c0 + 15]:
                           e.activation(out=o, in_=s, func=AF.Sqrt),
                           waits=[(s_dve, v_var)], inc=s_act)
            if i >= 1:
                ln_tail(i - 1)
            if i % 4 == 2 and i >= 6:
                emit_tblock((i - 6) // 4)
        ln_tail(NT - 1)
        emit_tblock(7)
        emit_qk_cast(1)
        XNT_ALL = (s_xt, s_xt.n)

        add("scalar", lambda e: e.preload_act_table(AF.Exp))
        v_gexp = [0] * NT
        v_sevac = [0] * NT
        v_sdma = [0] * NT
        v_strp = [0] * NT
        for i in range(NT):
            b = i % 2
            w = [(s_xt, v_xt[i]), (s_gt, 16)]
            if i >= 2:
                w.append((s_act, v_gexp[i - 2]))
            v_gmm = 0
            for d in range(ND):
                v_gmm = add("tensor", lambda e, lt=xnT[:, d * T + i * 128: d * T + (i + 1) * 128],
                            r=gate_sb[:, d * E:(d + 1) * E], st=(d == 0), sp=(d == ND - 1), pg=psG[i % 2]:
                            e.matmul(pg[:, 0:E], lhsT=lt, rhs=r, start=st, stop=sp),
                            waits=(w if d == 0 else ()), inc=(s_pe if d == ND - 1 else None))
            gc = 4 * i
            v_ngmx = add("vector", lambda e, o=gsc[:, gc:gc + 1], pg=psG[i % 2]: e.reduce_max(o, pg[:, 0:E], axis=AX.X, negate=True),
                         waits=[(s_pe, v_gmm)], inc=s_dve)
            v_gexp[i] = add("scalar", lambda e, o=gp2[:, b * 8:(b + 1) * 8], bi=gsc[:, gc:gc + 1], pg=psG[i % 2]:
                            e.activation(out=o, in_=pg[:, 0:E], func=AF.Exp, bias=bi),
                            waits=[(s_dve, v_ngmx)], inc=s_act)
            v_den = add("vector", lambda e, o=gsc[:, gc + 1:gc + 2], s=gp2[:, b * 8:(b + 1) * 8]:
                        e.reduce_sum(o, s, axis=AX.X), waits=[(s_act, v_gexp[i])], inc=s_dve)
            v_rden = add("vector", lambda e, o=gsc[:, gc + 2:gc + 3], s=gsc[:, gc + 1:gc + 2]: e.reciprocal(o, s),
                         waits=[(s_dve, v_den)], inc=s_dve)
            v_probs = add("vector", lambda e, o=gp2[:, b * 8:(b + 1) * 8], s=gp2[:, b * 8:(b + 1) * 8], r=gsc[:, gc + 2:gc + 3]:
                          e.tensor_scalar(out=o, in0=s, scalar1=r, scalar2=None, op0=ALU.mult),
                          waits=[(s_dve, v_rden)], inc=s_dve)
            v_t8 = add("vector", lambda e, o=t82[:, b * 8:(b + 1) * 8], s=gp2[:, b * 8:(b + 1) * 8]: e.max(out=o, in_=s),
                       waits=[(s_dve, v_probs)], inc=s_dve)
            w = [(s_dve, v_t8)]
            if i >= 2:
                w.append((s_pe, v_strp[i - 2]))
            v_sc = add("vector", lambda e, o=sc2[:, b * 8:(b + 1) * 8], s=gp2[:, b * 8:(b + 1) * 8], m2=t82[:, b * 8 + 1:b * 8 + 2]:
                       e.scalar_tensor_tensor(out=o, in0=s, scalar=m2, in1=s, op0=ALU.is_ge, op1=ALU.mult),
                       waits=w, inc=s_dve)
            w = [(s_dve, v_sc), CST]
            if i >= 1:
                w.append((s_dve, v_sevac[i - 1]))
            v_strp[i] = add("tensor", lambda e, s=sc2[:, b * 8:(b + 1) * 8]: e.transpose(psS[0:8, 0:128], s, idf_sb[:, :]),
                            waits=w, inc=s_pe)
            w = [(s_pe, v_strp[i])]
            if i >= 2:
                w.append((s_sf[b], v_sdma[i - 2]))
            v_sevac[i] = add("vector", lambda e, o=sst2[0:8, b * 128:(b + 1) * 128]: e.tensor_copy(o, psS[0:8, 0:128]),
                             waits=w, inc=s_dve)
            v_sdma[i] = add("sync", lambda e, s=sst2[0:8, b * 128:(b + 1) * 128], i_=i:
                            e.dma_start(out=sfm_h[:, i_ * 128:(i_ + 1) * 128], in_=s),
                            waits=[(s_dve, v_sevac[i])], inc=s_sf[b], k=16)
        SFM_ALL = [(s_sf[0], s_sf[0].n), (s_sf[1], s_sf[1].n)]

        v_vevac = [0] * NT
        v_vmm = [0] * NT
        for i in range(NT):
            b = i % 2
            w0 = [(s_wv, 16), (s_xt, v_xt[i])]
            if i >= 2:
                w0.append((s_dve, v_vevac[i - 2]))
            for half in range(2):
                for d in range(ND):
                    v_vmm[i] = add("tensor", lambda e, o=psV[b][:, half * 512:(half + 1) * 512],
                                   lt=xnT[:, d * T + i * 128: d * T + (i + 1) * 128],
                                   r=wvb[:, d * 1024 + half * 512: d * 1024 + (half + 1) * 512],
                                   st=(d == 0), sp=(d == ND - 1):
                                   e.matmul(o, lhsT=lt, rhs=r, start=st, stop=sp),
                                   waits=(w0 if (half == 0 and d == 0) else ()),
                                   inc=(s_pe if (half == 1 and d == ND - 1) else None))
            va = v_aug[:, i * 1040:(i + 1) * 1040].rearrange("p (h c) -> p h c", c=65)
            add("vector", lambda e, o=va[:, :, 64:65]: e.memset(o, 1.0))
            v_vevac[i] = add("vector", lambda e, o=va[:, :, 0:64],
                             s=psV[b][:, :].rearrange("p (h k) -> p h k", k=64):
                             e.tensor_copy(o, s), waits=[(s_pe, v_vmm[i])], inc=s_dve)
        v_vaug_done = s_dve.n

        barrier()
        p012.close()

        p3 = ExitStack()
        qT_sb = p3.enter_context(nc.sbuf_tensor("qT_sb", [128, T], BF16))
        kT_sb = p3.enter_context(nc.sbuf_tensor("kT_sb", [128, T], BF16))
        PT2 = p3.enter_context(nc.sbuf_tensor("PT2", [128, 3 * 1024], BF16))
        R2 = p3.enter_context(nc.sbuf_tensor("R2", [64, 2 * 512], F32))
        pv_sb = p3.enter_context(nc.sbuf_tensor("pv_sb", [65, 4 * 512], F32))
        recl_sb = p3.enter_context(nc.sbuf_tensor("recl_sb", [1, 4 * 512], F32))
        ao2 = p3.enter_context(nc.sbuf_tensor("ao2", [64, 2 * 512], BF16))

        def emit_bounces():
            for e_ in range(E):
                add("gpsimd", lambda e, ee=e_: e.dma_start(out=w13b_h[ee], in_=w13_h[ee]), inc=s_bnc, k=16)
                add("gpsimd", lambda e, ee=e_: e.dma_start(out=w2b_h[ee], in_=w2_h[ee]), inc=s_bnc, k=16)
        BNC = (s_bnc, 16 * 2 * E)

        v_qk_last_read = 0
        v_mult_last = 0
        v_cp_last = 0
        v_ao_dma = {}
        v_multR = {}
        v_tail_last = {}
        v_r1_by_pvb = {}
        last_exp_on_buf = {}
        last_pv_on_buf = {}

        for c in range(NH // 2):
            gA, gB = 2 * c, 2 * c + 1
            if 1 <= c < NH // 2 - 1:
                emit_qk_cast(c + 1, waits=[(s_pe, v_qk_last_read)])
                if c == 1:
                    emit_bounces()
            v_wk = v_wk_pair[c]
            bqk = (c % 2) * 1024

            with nc.psum_tensor("psQ0_%d" % c, [128, 512], F32) as psQ0, \
                 nc.psum_tensor("psQ1_%d" % c, [128, 512], F32) as psQ1:
                psQ = [psQ0, psQ1]
                v_qevac = {}
                for (which, wsb, src, scl) in (("q", qT_sb, wqp, 0.125), ("k", kT_sb, wkp, 1.0)):
                    for tb in range(NB):
                        bq = tb % 2
                        w = [(s_qk, v_wk), XNT_ALL]
                        if (which, bq) in v_qevac:
                            w.append((s_act, v_qevac[(which, bq)]))
                        elif which == "k" and ("q", bq) in v_qevac:
                            w.append((s_act, v_qevac[("q", bq)]))
                        if c > 0 and which == "q" and tb < 2:
                            w.append((s_dve, v_mult_last))
                        vm = 0
                        for d in range(ND):
                            vm = add("tensor", lambda e, o=psQ[bq][:, :], lt=src[:, bqk + d * 128:bqk + (d + 1) * 128],
                                     r=xnT[:, d * T + tb * 512: d * T + (tb + 1) * 512],
                                     st=(d == 0), sp=(d == ND - 1):
                                     e.matmul(o, lhsT=lt, rhs=r, start=st, stop=sp),
                                     waits=(w if d == 0 else ()), inc=(s_pe if d == ND - 1 else None))
                        v_qevac[(which, bq)] = add("scalar", lambda e, o=wsb[:, tb * 512:(tb + 1) * 512], s=psQ[bq][:, :], sc=scl:
                                                   e.activation(out=o, in_=s, func=AF.Copy, scale=sc),
                                                   waits=[(s_pe, vm)], inc=s_act)
                v_qk_ready = max(v_qevac.values())

            with nc.psum_tensor("psST0_%d" % c, [128, 1024], F32) as psST0, \
                 nc.psum_tensor("psST1_%d" % c, [128, 1024], F32) as psST1, \
                 nc.psum_tensor("psST2_%d" % c, [128, 1024], F32) as psST2, \
                 nc.psum_tensor("psPVA_%d" % c, [65, 512], F32) as psPVA, \
                 nc.psum_tensor("psPVB_%d" % c, [65, 512], F32) as psPVB:
                psST = [psST0, psST1, psST2]
                for j in range(NB):
                    ks = 4 * j + 4
                    v_st = [0] * ks
                    v_exp = [0] * ks
                    v_gate_pv = [0] * ks
                    v_pv = [0] * ks

                    def emit_st(k):
                        bs = k % 3
                        w = [] if (j > 0 or k >= 3) else [(s_act, v_qk_ready), CST]
                        if bs in last_exp_on_buf:
                            w.append((s_act, last_exp_on_buf[bs]))
                        diag = k >= 4 * j
                        r_ = k - 4 * j
                        add("tensor", lambda e, o=psST[bs][:, 0:512], lt=kT_sb[0:64, k * 128:(k + 1) * 128],
                            r=qT_sb[0:64, j * 512:(j + 1) * 512], sp=not diag:
                            e.matmul(o, lhsT=lt, rhs=r, start=True, stop=sp, tile_position=(0, 0)), waits=w)
                        v_stA = add("tensor", lambda e, o=psST[bs][:, 512:1024], lt=kT_sb[64:128, k * 128:(k + 1) * 128],
                                    r=qT_sb[64:128, j * 512:(j + 1) * 512], sp=not diag:
                                    e.matmul(o, lhsT=lt, rhs=r, start=True, stop=sp, tile_position=(64, 0)),
                                    inc=(None if diag else s_pe))
                        if diag:
                            add("tensor", lambda e, o=psST[bs][:, 0:512], m=cm_sb[:, r_ * 512:(r_ + 1) * 512]:
                                e.matmul(o, lhsT=idb_sb[:, :], rhs=m, start=False, stop=True))
                            v_stA = add("tensor", lambda e, o=psST[bs][:, 512:1024], m=cm_sb[:, r_ * 512:(r_ + 1) * 512]:
                                        e.matmul(o, lhsT=idb_sb[:, :], rhs=m, start=False, stop=True), inc=s_pe)
                        v_st[k] = v_stA

                    for kk in range(min(3, ks)):
                        emit_st(kk)
                    for k in range(ks):
                        bs = k % 3
                        w = [(s_pe, v_st[k])]
                        if bs in last_pv_on_buf:
                            w.append((s_pe, last_pv_on_buf[bs]))
                        v_exp[k] = add("scalar", lambda e, o=PT2[:, bs * 1024:(bs + 1) * 1024], s=psST[bs][:, :]:
                                       e.activation(out=o, in_=s, func=AF.Exp), waits=w, inc=s_act)
                        last_exp_on_buf[bs] = v_exp[k]
                        w = [(s_act, v_exp[k])]
                        if k == 0:
                            w.append((s_dve, v_cp_last))
                            w.append((s_dve, v_vaug_done))
                        add("tensor", lambda e, o=psPVA[:, :], lt=v_aug[:, k * 1040 + 65 * gA: k * 1040 + 65 * gA + 65],
                            r=PT2[:, bs * 1024: bs * 1024 + 512], st=(k == 0), sp=(k == ks - 1):
                            e.matmul(o, lhsT=lt, rhs=r, start=st, stop=sp), waits=w)
                        v_pv[k] = add("tensor", lambda e, o=psPVB[:, :], lt=v_aug[:, k * 1040 + 65 * gB: k * 1040 + 65 * gB + 65],
                                      r=PT2[:, bs * 1024 + 512: bs * 1024 + 1024], st=(k == 0), sp=(k == ks - 1):
                                      e.matmul(o, lhsT=lt, rhs=r, start=st, stop=sp), inc=s_pe)
                        last_pv_on_buf[bs] = v_pv[k]
                        if k + 3 < ks:
                            emit_st(k + 3)

                    v_cp = {}
                    for (g_, psPV, aob) in ((gA, psPVA, 0), (gB, psPVB, 1)):
                        pvb = (j % 2) * 2 + aob
                        w = [(s_pe, v_pv[ks - 1])]
                        if pvb in v_tail_last:
                            w.append((s_dve, v_tail_last[pvb]))
                        v_cp[aob] = add("vector", lambda e, o=pv_sb[:, pvb * 512:(pvb + 1) * 512], p=psPV:
                                        e.tensor_copy(o, p[:, :]), waits=w, inc=s_dve)
                    v_cp_last = max(v_cp.values())
                    for (g_, psPV, aob) in ((gA, psPVA, 0), (gB, psPVB, 1)):
                        pvb = (j % 2) * 2 + aob
                        w = [(s_dve, v_cp[aob])]
                        if pvb in v_r1_by_pvb:
                            w.append((s_rb[aob], v_r1_by_pvb[pvb]))
                        v_rc = add("vector", lambda e, o=recl_sb[0:1, pvb * 512:(pvb + 1) * 512],
                                   p=pv_sb[64:65, pvb * 512:(pvb + 1) * 512]:
                                   e.reciprocal(o, p), waits=w, inc=s_dve)
                        w = [(s_dve, v_rc)]
                        if aob in v_multR:
                            w.append((s_dve, v_multR[aob]))
                        v_r1 = add("sync", lambda e, r=recl_sb[0:1, pvb * 512:(pvb + 1) * 512], ab=aob:
                                   e.dma_start(out=rl_h[ab:ab + 1, :], in_=r), waits=w, inc=s_rb[aob], k=16)
                        v_r1_by_pvb[pvb] = v_r1
                        v_rdma = add("sync", lambda e, o=R2[:, aob * 512:(aob + 1) * 512], ab=aob:
                                     e.dma_start(out=o, in_=rl_h[ab, :].partition_broadcast(64)),
                                     waits=[(s_rb[aob], v_r1)], inc=s_rb[aob], k=16)
                        w = [(s_rb[aob], v_rdma)]
                        if aob in v_ao_dma:
                            w.append((s_ao[aob], v_ao_dma[aob]))
                        v_multR[aob] = add("vector", lambda e, o=ao2[:, aob * 512:(aob + 1) * 512],
                                           p=pv_sb[0:64, pvb * 512:(pvb + 1) * 512], r=R2[:, aob * 512:(aob + 1) * 512]:
                                           e.tensor_mul(o, p, r), waits=w, inc=s_dve)
                        v_mult_last = v_multR[aob]
                        v_tail_last[pvb] = v_multR[aob]
                        v_ao_dma[aob] = add("sync", lambda e, s=ao2[:, aob * 512:(aob + 1) * 512], g2=g_, j_=j:
                                            e.dma_start(out=atT_h[g2 * 64:(g2 + 1) * 64, j_ * 512:(j_ + 1) * 512], in_=s),
                                            waits=[(s_dve, v_multR[aob])], inc=s_ao[aob], k=16)
            v_qk_last_read = s_pe.n
        AO_ALL = [(s_ao[0], s_ao[0].n), (s_ao[1], s_ao[1].n)]

        barrier()
        p3.close()
        v_aug_cm.__exit__(None, None, None)
        wkp_cm.__exit__(None, None, None)
        wqp_cm.__exit__(None, None, None)

        p5 = ExitStack()
        w13t = p5.enter_context(nc.sbuf_tensor("w13t", [128, 2 * ND * 1024], BF16))
        g_sb = p5.enter_context(nc.sbuf_tensor("g_sb", [128, E * 4 * 512], BF16))
        wob = p5.enter_context(nc.sbuf_tensor("wob", [128, ND * 1024], BF16))
        S_bc = p5.enter_context(nc.sbuf_tensor("S_bc", [128, 3 * 512], F32))
        xs_sb = p5.enter_context(nc.sbuf_tensor("xs_sb", [128, 2 * ND * 512], BF16))
        sa_sb = p5.enter_context(nc.sbuf_tensor("sa_sb", [128, 2 * 512], F32))
        w2t = p5.enter_context(nc.sbuf_tensor("w2t", [128, 2 * 2048], BF16))
        at5 = p5.enter_context(nc.sbuf_tensor("at5", [128, 2 * 1024], BF16))
        x5 = p5.enter_context(nc.sbuf_tensor("x5", [128, 4 * 512], F32))
        o5 = p5.enter_context(nc.sbuf_tensor("o5", [128, 4 * 512], F32))

        psU1 = [p5.enter_context(nc.psum_tensor("psU1%d" % s, [128, 512], F32)) for s in range(2)]
        psU2 = [p5.enter_context(nc.psum_tensor("psU2%d" % s, [128, 512], F32)) for s in range(2)]
        psY = [p5.enter_context(nc.psum_tensor("psY%d" % s, [128, 512], F32)) for s in range(4)]

        add("scalar", lambda e: e.preload_act_table(AF.Silu))
        add("gpsimd", lambda e: e.dma_start(out=wob[:, :].rearrange("p (dc o) -> p dc o", o=1024), in_=woT_h[:, :].rearrange("(dc p) o -> p dc o", p=128)),
            inc=s_wo, k=16)

        v_w13t = {}
        v_u_last_on = {}
        v_sbc = {}
        v_xs = {}
        v_xs_reader = {}
        v_sbc_reader = {}
        v_g = {}
        v_silu = {}
        v_gdone_tb = 0
        v_oadd = {}
        v_outdma = {}
        v_w2dma = {}
        v_atdma = {}
        v_x5dma = {}
        v_w2_last_read = 0

        for tb in range(NB):
            bw = 0
            if tb == 0:
                v_w13t[0] = add("sync", lambda e: e.dma_start(
                    out=w13t[:, 0:8192].rearrange("p (dc hh) -> p dc hh", hh=1024),
                    in_=w13b_h[0].rearrange("(dc p) hh -> p dc hh", p=128)), waits=[BNC], inc=s_w13[0], k=16)

            def emit_sbc(ge2):
                ee, tb2 = ge2 % E, ge2 // E
                bs3 = ge2 % 3
                w = list(SFM_ALL)
                if bs3 in v_sbc_reader:
                    w.append((s_dve, v_sbc_reader[bs3]))
                v_sbc[bs3] = add("sync", lambda e, e2=ee, t0=tb2 * 512, o=S_bc[:, bs3 * 512:(bs3 + 1) * 512]:
                                 e.dma_start(out=o, in_=sfm_h[e2, t0:t0 + 512].partition_broadcast(128)),
                                 waits=w, inc=s_sbc[bs3], k=16)

            def emit_xs(ge2):
                ee, tb2 = ge2 % E, ge2 // E
                bs3 = ge2 % 3
                be = ge2 % 2
                vxs = 0
                for d in range(ND):
                    w = [(s_sbc[bs3], v_sbc[bs3]), XNT_ALL]
                    if d == 0 and be in v_xs_reader:
                        w.append((s_pe, v_xs_reader[be]))
                    vxs = add("vector", lambda e, o=xs_sb[:, (be * ND + d) * 512:(be * ND + d + 1) * 512],
                              xn_=xnT[:, d * T + tb2 * 512: d * T + (tb2 + 1) * 512],
                              sb_=S_bc[:, bs3 * 512:(bs3 + 1) * 512]:
                              e.tensor_mul(o, xn_, sb_), waits=(w if d == 0 else ()), inc=s_dve)
                v_xs[be] = vxs
                v_sbc_reader[bs3] = vxs

            if tb == 0:
                emit_sbc(0)
                emit_sbc(1)
                emit_xs(0)
            for e_ in range(E):
                bu = (e_ + bw) % 2
                bx = e_ % 2
                ge = tb * E + e_
                if ge + 2 < NB * E:
                    emit_sbc(ge + 2)
                if ge + 1 < NB * E:
                    emit_xs(ge + 1)
                ep = e_ + 1 if e_ + 1 < E else (0 if tb + 1 < NB else None)
                if ep is not None:
                    nb_ = ep % 2
                    w = [BNC]
                    if nb_ in v_u_last_on:
                        w.append((s_pe, v_u_last_on[nb_]))
                    v_w13t[nb_] = add("sync", lambda e, ee=ep, nb2=nb_: e.dma_start(
                        out=w13t[:, nb2 * 8192: nb2 * 8192 + 8192].rearrange("p (dc hh) -> p dc hh", hh=1024),
                        in_=w13b_h[ee].rearrange("(dc p) hh -> p dc hh", p=128)), waits=w, inc=s_w13[nb_], k=16)
                for h in range(4):
                    bh = h % 2
                    wu = [(s_w13[bu], v_w13t[bu]), (s_dve, v_xs[bx])]
                    if bh in v_silu:
                        wu.append((s_act, v_silu[bh]))
                    vu1 = 0
                    for d in range(ND):
                        vu1 = add("tensor", lambda e, o=psU1[bh][:, :], lt=w13t[:, bu * 8192 + d * 1024 + h * 128: bu * 8192 + d * 1024 + (h + 1) * 128],
                                  r=xs_sb[:, (bx * ND + d) * 512:(bx * ND + d + 1) * 512], st=(d == 0), sp=(d == ND - 1):
                                  e.matmul(o, lhsT=lt, rhs=r, start=st, stop=sp),
                                  waits=(wu if d == 0 else ()), inc=(s_pe if d == ND - 1 else None))
                    wu = []
                    if bh in v_g:
                        wu.append((s_dve, v_g[bh]))
                    vu3 = 0
                    for d in range(ND):
                        vu3 = add("tensor", lambda e, o=psU2[bh][:, :], lt=w13t[:, bu * 8192 + d * 1024 + 512 + h * 128: bu * 8192 + d * 1024 + 512 + (h + 1) * 128],
                                  r=xs_sb[:, (bx * ND + d) * 512:(bx * ND + d + 1) * 512], st=(d == 0), sp=(d == ND - 1):
                                  e.matmul(o, lhsT=lt, rhs=r, start=st, stop=sp),
                                  waits=(wu if d == 0 else ()), inc=(s_pe if d == ND - 1 else None))
                    v_u_last_on[bu] = vu3
                    v_xs_reader[bx] = vu3
                    w = [(s_pe, vu1)]
                    if bh in v_g:
                        w.append((s_dve, v_g[bh]))
                    v_silu[bh] = add("scalar", lambda e, o=sa_sb[:, bh * 512:(bh + 1) * 512], p=psU1[bh]:
                                     e.activation(out=o, in_=p[:, :], func=AF.Silu), waits=w, inc=s_act)
                    slot = (e_ * 4 + h)
                    w = [(s_act, v_silu[bh]), (s_pe, vu3)]
                    if v_w2_last_read and tb > 0:
                        w.append((s_pe, v_w2_last_read))
                    v_g[bh] = add("vector", lambda e, o=g_sb[:, slot * 512:(slot + 1) * 512],
                                  sa=sa_sb[:, bh * 512:(bh + 1) * 512], p=psU2[bh]:
                                  e.tensor_mul(o, sa, p[:, :]), waits=w, inc=s_dve)
            v_gdone_tb = s_dve.n
            if tb == 0 and "g0" in dbg_h:
                add("sync", lambda e: e.dma_start(out=dbg_h["g0"][:, :], in_=g_sb[:, :]),
                    waits=[(s_dve, v_gdone_tb)], inc=s_out, k=16)

            for half in range(2):
                vy_stop = [0] * 4
                for dq in range(4):
                    bat = dq % 2
                    w = list(AO_ALL)
                    if bat in v_atdma:
                        w.append((s_pe, v_atdma[bat][1]))
                    if dq == 0:
                        for s4 in range(4):
                            if s4 in v_oadd:
                                w.append((s_dve, v_oadd[s4]))
                    vdma = add("sync", lambda e, d0=dq * 2, o=at5[:, bat * 1024:(bat + 1) * 1024].rearrange("p (dc c) -> p dc c", c=512), t0=tb * 512:
                               e.dma_start(out=o, in_=atT_h[d0 * 128:(d0 + 2) * 128, t0:t0 + 512].rearrange("(dc p) c -> p dc c", p=128)),
                               waits=w, inc=s_at[bat], k=16)
                    vmm = 0
                    for di in range(2):
                        d = dq * 2 + di
                        for sub in range(4):
                            w = [(s_at[bat], vdma), (s_wo, 16)]
                            if d == 0 and sub in v_oadd:
                                w.append((s_dve, v_oadd[sub]))
                            vmm = add("tensor", lambda e, o=psY[sub][:, :],
                                      lt=at5[:, bat * 1024 + di * 512 + sub * 128: bat * 1024 + di * 512 + (sub + 1) * 128],
                                      r=wob[:, d * 1024 + half * 512: d * 1024 + (half + 1) * 512],
                                      st=(d == 0), sp=False:
                                      e.matmul(o, lhsT=lt, rhs=r, start=st, stop=sp),
                                      waits=(w if sub == 0 else ()), inc=(s_pe if (sub == 3 and di == 1) else None))
                    v_atdma[bat] = (vdma, vmm)
                for e_ in range(E):
                    bwt = e_ % 2
                    w = [BNC]
                    if bwt in v_w2dma:
                        w.append((s_pe, v_w2dma[bwt][1]))
                    vdma = add("sync", lambda e, ee=e_, hf=half, o=w2t[:, bwt * 2048:(bwt + 1) * 2048].rearrange("p (hh c) -> p hh c", c=512):
                               e.dma_start(out=o, in_=w2b_h[ee][:, hf * 512:(hf + 1) * 512].rearrange("(hh p) c -> p hh c", p=128)),
                               waits=w, inc=s_w2[bwt], k=16)
                    vmm = 0
                    for h in range(4):
                        slot = e_ * 4 + h
                        for sub in range(4):
                            w = [(s_w2[bwt], vdma), (s_dve, v_gdone_tb if (e_ == 0 and h == 0 and sub == 0) else 0)]
                            if e_ == E - 1 and h == 3:
                                vy_stop[sub] = 0
                            vmm = add("tensor", lambda e, o=psY[sub][:, :],
                                      lt=g_sb[:, slot * 512 + sub * 128: slot * 512 + (sub + 1) * 128],
                                      r=w2t[:, bwt * 2048 + h * 512: bwt * 2048 + (h + 1) * 512],
                                      st=False, sp=(e_ == E - 1 and h == 3):
                                      e.matmul(o, lhsT=lt, rhs=r, start=st, stop=sp),
                                      waits=(w if sub == 0 else ()), inc=(s_pe if (h == 3) else None))
                            if e_ == E - 1 and h == 3:
                                vy_stop[sub] = vmm
                    v_w2dma[bwt] = (vdma, vmm)
                v_w2_last_read = s_pe.n
                if tb == 0 and half == 0 and "y0" in dbg_h:
                    y0_sb = p5.enter_context(nc.sbuf_tensor("y0_sb", [128, 4 * 512], F32))
                    vy0 = 0
                    for s4 in range(4):
                        vy0 = add("vector", lambda e, o=y0_sb[:, s4 * 512:(s4 + 1) * 512], p_=psY[s4]:
                                  e.tensor_copy(o, p_[:, :]), waits=[(s_pe, vy_stop[s4])], inc=s_dve)
                    add("sync", lambda e: e.dma_start(out=dbg_h["y0"][:, :], in_=y0_sb[:, :]),
                        waits=[(s_dve, vy0)], inc=s_out, k=16)
                vx4 = [0] * 4
                for sub in range(4):
                    row0 = tb * 512 + sub * 128
                    w = []
                    if sub in v_x5dma:
                        w.append((s_dve, v_x5dma[sub]))
                    vx4[sub] = add("sync", lambda e, o=x5[:, sub * 512:(sub + 1) * 512], r0=row0, hf=half:
                                   e.dma_start(out=o, in_=x_h[r0:r0 + 128, hf * 512:(hf + 1) * 512]),
                                   waits=w, inc=s_x5[sub], k=16)
                for sub in range(4):
                    row0 = tb * 512 + sub * 128
                    w = [(s_pe, vy_stop[sub]), (s_x5[sub], vx4[sub])]
                    if sub in v_outdma:
                        w.append((s_o[sub], v_outdma[sub]))
                    v_oadd[sub] = add("vector", lambda e, o=o5[:, sub * 512:(sub + 1) * 512], s=sub, x_=x5[:, sub * 512:(sub + 1) * 512]:
                                      e.tensor_add(o, psY[s][:, :], x_), waits=w, inc=s_dve)
                    v_x5dma[sub] = v_oadd[sub]
                    v_outdma[sub] = add("sync", lambda e, s_=o5[:, sub * 512:(sub + 1) * 512], r0=row0, hf=half:
                                        e.dma_start(out=out_h[r0:r0 + 128, hf * 512:(hf + 1) * 512], in_=s_),
                                        waits=[(s_dve, v_oadd[sub])], inc=s_o[sub], k=16)

        if "xnT" in dbg_h:
            for cch in range(ND):
                add("sync", lambda e, c2=cch: e.dma_start(out=dbg_h["xnT"][c2 * 128:(c2 + 1) * 128, :],
                    in_=xnT[:, c2 * T:(c2 + 1) * T]), waits=[XNT_ALL], inc=s_out, k=16)
        if "scale" in dbg_h:
            add("sync", lambda e: e.dma_start(out=dbg_h["scale"][:, :], in_=sfm_h[:, :]),
                waits=SFM_ALL, inc=s_out, k=16)
        if "attnT" in dbg_h:
            add("sync", lambda e: e.dma_start(out=dbg_h["attnT"][:, :], in_=atT_h[:, :]),
                waits=AO_ALL, inc=s_out, k=16)

        add("sync", lambda e: e.nop(),
            waits=[(so, so.n) for so in s_o] + [(s_out, s_out.n)])

        block = ctx.enter_context(nc.Block())
        block.sync(lambda eng: [f(eng) for f in P["sync"]])
        block.gpsimd(lambda eng: [f(eng) for f in P["gpsimd"]])
        block.scalar(lambda eng: [f(eng) for f in P["scalar"]])
        block.vector(lambda eng: [f(eng) for f in P["vector"]])
        block.tensor(lambda eng: [f(eng) for f in P["tensor"]])

        p5.close()
    return nc


def _prep_inputs(L0_attn_x, L0_attn_Wq, L0_attn_Wk, L0_attn_Wv, L0_attn_Wo,
                 L0_ffn_gate_w, L0_ffn_w1, L0_ffn_w2, L0_ffn_w3):
    f32 = np.float32
    x = np.ascontiguousarray(np.asarray(L0_attn_x, f32))
    wqT = np.ascontiguousarray(np.asarray(L0_attn_Wq, f32).T)
    wkT = np.ascontiguousarray(np.asarray(L0_attn_Wk, f32).T)
    wvT = np.ascontiguousarray(np.asarray(L0_attn_Wv, f32).T)
    woT = np.ascontiguousarray(np.asarray(L0_attn_Wo, f32).T)
    gw = np.ascontiguousarray(np.asarray(L0_ffn_gate_w, f32))
    w13 = np.ascontiguousarray(np.concatenate(
        [np.asarray(L0_ffn_w1, f32), np.asarray(L0_ffn_w3, f32)], axis=2))
    w2 = np.ascontiguousarray(np.asarray(L0_ffn_w2, f32))

    idf = np.eye(128, dtype=f32)
    idb = np.eye(128, dtype=ml_dtypes.bfloat16)
    i_idx = np.arange(128)[:, None]
    j_idx = np.arange(512)[None, :]
    cm = np.zeros((128, 4 * 512), dtype=np.float32)
    for r in range(4):
        cm[:, r * 512:(r + 1) * 512] = np.where((128 * r + i_idx) > j_idx, -30000.0, 0.0)
    cm = cm.astype(ml_dtypes.bfloat16)
    ones = np.ones((1, 128), f32)

    shared = dict(wqT=wqT, wkT=wkT, wvT=wvT, woT=woT, gw=gw, w13=w13, w2=w2,
                  idf=idf, idb=idb, cmask=cm, ones=ones)
    maps = []
    for i in range(N_CORES):
        m = dict(shared)
        m["x"] = np.ascontiguousarray(x[i])
        maps.append(m)
    return maps


def kernel(L0_attn_x, L0_attn_Wq, L0_attn_Wk, L0_attn_Wv, L0_attn_Wo,
           L0_ffn_gate_w, L0_ffn_w1, L0_ffn_w2, L0_ffn_w3):
    maps = _prep_inputs(L0_attn_x, L0_attn_Wq, L0_attn_Wk, L0_attn_Wv, L0_attn_Wo,
                        L0_ffn_gate_w, L0_ffn_w1, L0_ffn_w2, L0_ffn_w3)
    nc = build()
    res = run_bass_kernel_spmd(nc, maps, core_ids=list(range(N_CORES)))
    return np.stack([np.asarray(res.results[i]["out"]) for i in range(N_CORES)], axis=0)
